# version 03 (final)
# baseline (speedup 1.0000x reference)
import numpy as np
from contextlib import ExitStack

import concourse.bass as bass
import concourse.mybir as mybir
from concourse.bass_utils import run_bass_kernel_spmd

F32 = mybir.dt.float32
BF16 = mybir.dt.bfloat16
AF = mybir.ActivationFunctionType
ALU = mybir.AluOpType
AX = mybir.AxisListType

T = 1024
NT = 8
DM = 2048
NKD = 16
EPS = 1e-5
NEG = -1e30

H = 32
HP = 16
PD = 64
ND = 64
DI = 2048

NE = 32
EM = 768
NKE = 6

BI = 4096
NKB = 32

CONST_VALS = (1e-5, -0.5, 1.0, 0.0, -1.0)

DBG = {}


class EQ:

    def __init__(self, name, inc_amount=1):
        self.name = name
        self.inc_amount = inc_amount
        self.ops = []
        self.n = 0
        self.waited = {}

    def bind(self, sem):
        self.sem = sem

    def wait(self, other, tag):
        if tag <= 0:
            return
        if self.waited.get(other.name, 0) >= tag:
            return
        self.waited[other.name] = tag

        def f(e, other=other, tag=tag):
            e.wait_ge(other.sem, tag)
        self.ops.append(f)

    def op(self, f):
        self.ops.append(f)

    def opi(self, f):
        self.n += self.inc_amount
        tag = self.n

        def g(e, f=f):
            f(e).then_inc(self.sem, self.inc_amount)
        self.ops.append(g)
        return tag

    def wait_dma(self, dq):
        self.wait(dq, dq.n)

    def self_edge(self):
        tag = self.n
        if self.waited.get(self.name, 0) >= tag:
            return

        def f(e, tag=tag):
            e.wait_ge(self.sem, tag)
        self.ops.append(f)
        self.waited[self.name] = tag


class KB:
    def __init__(self, nc, ctx, block):
        self.nc = nc
        self.ctx = ctx
        self.block = block
        self.TE = EQ("TE")
        self.DV = EQ("DV")
        self.AC = EQ("AC")
        self.SY = EQ("SY", 16)
        self.GP = EQ("GP", 16)
        self.queues = [self.TE, self.DV, self.AC, self.SY, self.GP]

    def sb(self, name, shape, dtype):
        return self.ctx.enter_context(self.nc.sbuf_tensor(name, shape, dtype))

    def ctx_sem(self, name):
        self._nsem = getattr(self, "_nsem", 0) + 1
        return self.ctx.enter_context(
            self.nc.semaphore(f"ws{self._nsem}_{name}"))

    def ps(self, name, shape, dtype=F32):
        return self.ctx.enter_context(self.nc.psum_tensor(name, shape, dtype))

    def barrier(self):
        tags = [(q, q.n) for q in self.queues]
        for q in self.queues:
            for oq, t in tags:
                if oq is not q:
                    q.wait(oq, t)

    def flush(self):
        b = self.block
        if self.TE.ops:
            ops = self.TE.ops
            b.tensor(lambda e, ops=ops: [f(e) for f in ops])
        if self.DV.ops:
            ops = self.DV.ops
            b.vector(lambda e, ops=ops: [f(e) for f in ops])
        if self.AC.ops:
            ops = self.AC.ops
            b.scalar(lambda e, ops=ops: [f(e) for f in ops])
        if self.SY.ops:
            ops = self.SY.ops
            b.sync(lambda e, ops=ops: [f(e) for f in ops])
        if self.GP.ops:
            ops = self.GP.ops
            b.gpsimd(lambda e, ops=ops: [f(e) for f in ops])
        for q in self.queues:
            q.ops = []


class SemBox:

    def __init__(self, name, sem):
        self.name = name
        self.sem = sem
        self.n = 0


class WStream:

    def __init__(self, kb, name, shape, dtype=BF16, n_bufs=2):
        self.kb = kb
        self.bufs = [kb.ctx_sb(f"{name}{i}", shape, dtype) for i in range(n_bufs)]
        self.sems = [SemBox(f"{name}s{i}", kb.ctx_sem(f"{name}s{i}"))
                     for i in range(n_bufs)]
        self.n_bufs = n_bufs
        self.i = 0
        self.free_tag = [(None, 0)] * n_bufs

    def load(self, src_ap):
        slot = self.i % self.n_bufs
        self.i += 1
        q, t = self.free_tag[slot]
        if q is not None:
            self.kb.GP.wait(q, t)
        buf = self.bufs[slot]
        box = self.sems[slot]
        box.n += 16
        tag = box.n

        def f(e, b=buf, s=src_ap, box=box):
            e.dma_start(out=b[:], in_=s).then_inc(box.sem, 16)
        self.kb.GP.op(f)
        return (box, tag), buf, slot

    def consumed(self, slot, queue, tag):
        self.free_tag[slot] = (queue, tag)


def dbg_dump(kb, name, ap, shape, dtype):
    out = kb.nc.dram_tensor("dbg_" + name, shape, dtype,
                            kind="ExternalOutput").ap()
    kb.barrier()
    st = kb.SY.opi(lambda e, out=out, ap=ap: e.dma_start(out=out, in_=ap))
    kb.SY.op(lambda e, st=st: e.wait_ge(kb.SY.sem, st))


def emit_ln(kb, C, x_src3, t_tiles, xn_dst, dst_col, x_ready,
            sq_out, stats, ssqt, pbt, pbt_state):
    TE, DV, AC = kb.TE, kb.DV, kb.AC
    q0, t0 = x_ready
    AC.wait(kb.DV, kb.DV.n)
    AC.wait(q0, t0)
    DV.wait(q0, t0)
    AC.wait_dma(kb.SY)
    DV.wait_dma(kb.SY)

    ntl = len(t_tiles)
    ssq_tags = []
    for i, tt in enumerate(t_tiles):
        xt = x_src3[:, tt, :]
        ssq_tags.append(AC.opi(lambda e, xt=xt, i=i: e.activation(
            out=sq_out[:], in_=xt, func=AF.Square,
            accum_out=ssqt[:, i:i+1])))
    for i, tt in enumerate(t_tiles):
        xt = x_src3[:, tt, :]
        DV.opi(lambda e, xt=xt, i=i: e.reduce_sum(
            stats[:, 4*i:4*i+1], xt, axis=AX.X))
    DV.self_edge()
    for i in range(ntl):
        DV.opi(lambda e, i=i: e.tensor_scalar(
            stats[:, 4*i+2:4*i+3], stats[:, 4*i:4*i+1], 1.0/DM, None,
            op0=ALU.mult))
    DV.self_edge()
    for i in range(ntl):
        DV.opi(lambda e, i=i: e.tensor_mul(
            stats[:, 4*i:4*i+1], stats[:, 4*i+2:4*i+3], stats[:, 4*i+2:4*i+3]))
    DV.self_edge()
    rstd_tags = []
    vt = 0
    for i in range(ntl):
        DV.wait(AC, ssq_tags[i])
        vt = DV.opi(lambda e, i=i: e.scalar_tensor_tensor(
            stats[:, 4*i+1:4*i+2], ssqt[:, i:i+1], 1.0/DM,
            stats[:, 4*i:4*i+1], op0=ALU.mult, op1=ALU.subtract))
    for i in range(ntl):
        AC.wait(DV, vt)
        AC.opi(lambda e, i=i: e.activation(
            out=stats[:, 4*i+1:4*i+2], in_=stats[:, 4*i+1:4*i+2],
            func=AF.Ln, bias=EPS))
    AC.self_edge()
    for i in range(ntl):
        rstd_tags.append(AC.opi(lambda e, i=i: e.activation(
            out=stats[:, 4*i+3:4*i+4], in_=stats[:, 4*i+1:4*i+2],
            func=AF.Exp, scale=-0.5)))
    if DBG.get("mode") == "ln0":
        dbg_dump(kb, "st0", stats.ap(), [128, 32], F32)
        dbg_dump(kb, "ssq0", ssqt.ap(), [128, 8], F32)
        kb.flush()
        return -1
    last_dv = 0
    for i, tt in enumerate(t_tiles):
        xt = x_src3[:, tt, :]
        DV.wait(AC, rstd_tags[i])
        DV.self_edge()
        xn_tag = DV.opi(lambda e, xt=xt, i=i: e.tensor_scalar(
            C.xn_tok[:], xt, stats[:, 4*i+2:4*i+3], stats[:, 4*i+3:4*i+4],
            op0=ALU.subtract, op1=ALU.mult))
        col = dst_col(tt)
        for b in range(4):
            q, t = pbt_state.get("free", (None, 0))
            if q is not None:
                TE.wait(q, t)
            TE.wait(DV, xn_tag)
            tr = 0
            for j in range(4):
                k = 4*b + j
                tr = TE.opi(lambda e, k=k, j=j: e.transpose(
                    pbt[:, 128*j:128*j+128], C.xn_tok[:, 128*k:128*k+128],
                    C.idb[:]))
            DV.wait(TE, tr)
            cp = DV.opi(lambda e, b=b, col=col: e.tensor_copy(
                xn_dst[:, 4*b:4*b+4, col:col+128],
                pbt.ap().rearrange("p (j c) -> p j c", j=4)))
            pbt_state["free"] = (DV, cp)
            last_dv = cp
        DV.wait(TE, tr)
    return last_dv


def emit_attn_a(kb, C):
    nc = kb.nc
    TE, DV, AC, SY, GP = kb.TE, kb.DV, kb.AC, kb.SY, kb.GP

    with ExitStack() as actx:
        def sb(name, shape, dtype):
            return actx.enter_context(nc.sbuf_tensor("a_" + name, shape, dtype))
        kb.ctx_sb = sb

        xnq = sb("xnq", [128, NKD, 256], BF16)
        gb = sb("gb", [128, DI], BF16)
        bb = sb("bb", [128, DI], BF16)
        Db = sb("Db", [128, DI], BF16)
        S_f = sb("S_f", [128, HP, ND], F32)
        S_b = sb("S_b", [128, HP, ND], BF16)
        xp3 = sb("xp3", [128, 2, DM], F32)
        xt_q = sb("xt_q", [128, 2, DI], BF16)
        zt_q = sb("zt_q", [128, 2, DI], BF16)
        b_q = sb("b_q", [128, HP, 256], BF16)
        c_q = sb("c_q", [128, HP, 256], BF16)
        yn_q = sb("yn_q", [128, NKD, 256], BF16)
        ea_q = sb("ea_q", [32, 256], F32)
        y_sb = sb("y_sb", [128, DI], F32)
        dtmp = sb("dtmp", [128, DI], BF16)
        lnsq = sb("lnsq", [128, DM], BF16)
        g_sb = sb("g_sb", [128, 32], F32)
        ea_tok = sb("ea_tok", [128, 32], F32)
        w_sb = sb("w_sb", [128, 32], F32)
        eg_sb = sb("eg_sb", [128, 32], F32)
        eG_sb = sb("eG_sb", [128, 32], F32)
        gs_tile = sb("gs_tile", [128, 128], F32)
        dec_sb = [sb(f"dec{i}", [128, 128], F32) for i in range(2)]
        smt_sb = [sb(f"smt{i}", [128, 128], BF16) for i in range(2)]
        bT_sb = sb("bT", [128, 128], BF16)
        xw_sb = [sb(f"xw{i}", [128, 64], BF16) for i in range(2)]
        t64 = [sb(f"t64_{i}", [128, 64], F32) for i in range(2)]
        stats = sb("lnstat", [128, 32], F32)
        ssqt = sb("ssqt", [128, 8], F32)

        wsA = WStream(kb, "wsA", [128, 8, 512], BF16, 3)
        wsB = WStream(kb, "wsB", [128, 8, 256], BF16, 2)
        wsL = WStream(kb, "wsL", [128, 16, 32], BF16, 1)

        def ps(name, shape, dtype=F32):
            return actx.enter_context(nc.psum_tensor("a_" + name, shape, dtype))
        pxz = [ps(f"pxz{i}", [128, 512]) for i in range(2)]
        paux = ps("paux", [128, 512])
        psc = ps("psc", [128, 128])
        pgt = ps("pgt", [128, 128])
        py = ps("py", [128, 256])
        pds = ps("pds", [128, 64])
        pbt = ps("pbt", [128, 512], BF16)

        pxz_free = [(None, 0), (None, 0)]
        pbt_state = {}
        paux_free = [None]
        py_free = [None]
        psc_free = [None]
        pgt_free = [None]
        pds_free = [None]
        sbf = {}

        winaT3 = C.winaT.rearrange("(kk p) m -> p kk m", p=128)
        woutaT3 = C.woutaT.rearrange("(kk p) m -> p kk m", p=128)

        C.idf = sb("idf", [128, 128], F32)
        C.uneg = sb("uneg", [128, 128], F32)
        C.negones = sb("negones", [128, 128], F32)
        C.maskt = sb("maskt", [128, 128], F32)
        for (csrc, dst) in ((C.c_idf, C.idf), (C.c_uneg, C.uneg),
                            (C.c_negones, C.negones), (C.c_maskt, C.maskt)):
            SY.opi(lambda e, csrc=csrc, dst=dst: e.dma_start(
                out=dst[:], in_=csrc))
        for (src, dst) in ((C.gammaB, gb), (C.betaB, bb), (C.DaB, Db)):
            gbt = GP.opi(lambda e, src=src, dst=dst: e.dma_start(
                out=dst[:], in_=src))
        gbd_tag = gbt

        DV.opi(lambda e: e.memset(S_f.ap().rearrange("p h n -> p (h n)"), 0.0))
        sz = DV.opi(lambda e: e.memset(S_b.ap().rearrange("p h n -> p (h n)"), 0.0))
        S_ready = [sz]

        own_list = (True,) if DBG.get("skip_partner") else (False, True)
        nq = DBG.get("quarters", 4)
        for own in own_list:
            for q4 in range(nq):
                if own:
                    ln_tiles = [2*q4, 2*q4+1]
                    xv3 = C.x_res
                    x_ready = (SY, kb.SY.n)
                    def dcol(tt, q4=q4):
                        return (tt - 2*q4) * 128
                else:
                    qx = sbf.get("xp3", (None, 0))
                    if qx[0] is not None:
                        SY.wait(*qx)
                    lt = SY.opi(lambda e, q4=q4: e.dma_start(
                        out=xp3[:],
                        in_=C.x_part.rearrange("(tt p) d -> p tt d", p=128)[
                            :, 2*q4:2*q4+2, :]))
                    x_ready = (SY, lt)
                    ln_tiles = [0, 1]
                    xv3 = xp3
                    def dcol(tt):
                        return tt * 128
                xn_tag = emit_ln(kb, C, xv3, ln_tiles, xnq, dcol,
                                 x_ready, lnsq, stats, ssqt, pbt, pbt_state)
                if xn_tag == -1:
                    return
                if not own:
                    sbf["xp3"] = (TE, kb.TE.n)

                n_list = list(range(4)) + (list(range(4, 8)) if own else [])
                for n in n_list:
                    subs = [wsA.load(winaT3[:, 8*kh:8*kh+8, 512*n:512*n+512])
                            for kh in range(2)]
                    for tl in range(2):
                        bnk = tl % 2
                        qf = pxz_free[bnk]
                        if qf[0] is not None:
                            TE.wait(*qf)
                        TE.wait(DV, xn_tag)
                        mt = 0
                        for k in range(16):
                            kh, kj = k // 8, k % 8
                            tag, buf, slot = subs[kh]
                            TE.wait(*tag)
                            mt = TE.opi(lambda e, k=k, kj=kj, buf=buf, tl=tl,
                                        bnk=bnk: e.matmul(
                                pxz[bnk][:], xnq[:, k, 128*tl:128*tl+128],
                                buf[:, kj, :], start=(k == 0), stop=(k == 15)))
                        if n < 4:
                            DV.wait(TE, mt)
                            et = DV.opi(lambda e, tl=tl, n=n, bnk=bnk: e.tensor_copy(
                                xt_q[:, tl, 512*n:512*n+512], pxz[bnk][:]))
                            pxz_free[bnk] = (DV, et)
                        else:
                            AC.wait(TE, mt)
                            et = AC.opi(lambda e, tl=tl, n=n, bnk=bnk: e.activation(
                                out=zt_q[:, tl, 512*(n-4):512*(n-4)+512],
                                in_=pxz[bnk][:], func=AF.Silu))
                            pxz_free[bnk] = (AC, et)
                    for tag, buf, slot in subs:
                        wsA.consumed(slot, TE, kb.TE.n)
                xz_dv = kb.DV.n
                xz_ac = kb.AC.n

                m_pairs = list(range(0, 16, 2)) + (list(range(16, 32, 2))
                                                   if own else [])
                for m0 in m_pairs:
                    subs = [wsB.load(
                        winaT3[:, 8*kh:8*kh+8, 4096+128*m0:4096+128*m0+256])
                        for kh in range(2)]
                    for mi in range(2):
                        m = m0 + mi
                        bnk = m % 2
                        qf = pxz_free[bnk]
                        if qf[0] is not None:
                            TE.wait(*qf)
                        mt = 0
                        for k in range(16):
                            kh, kj = k // 8, k % 8
                            tag, buf, slot = subs[kh]
                            TE.wait(*tag)
                            mt = TE.opi(lambda e, kj=kj, k=k, buf=buf, bnk=bnk,
                                        mi=mi: e.matmul(
                                pxz[bnk][:, 0:256], buf[:, kj, 128*mi:128*mi+128],
                                xnq[:, k, 0:256], start=(k == 0), stop=(k == 15)))
                        DV.wait(TE, mt)
                        dst = b_q if m < 16 else c_q
                        et = DV.opi(lambda e, dst=dst, m=m % 16, bnk=bnk:
                                    e.tensor_copy(
                            dst[:, m, :], pxz[bnk][:, 0:256]))
                        pxz_free[bnk] = (DV, et)
                    for tag, buf, slot in subs:
                        wsB.consumed(slot, TE, kb.TE.n)
                bc_dv = kb.DV.n

                tag, buf, slot = wsL.load(winaT3[:, :, 8192:8224])
                if paux_free[0] is not None:
                    TE.wait(*paux_free[0])
                TE.wait(*tag)
                mt = 0
                for k in range(16):
                    mt = TE.opi(lambda e, k=k, buf=buf: e.matmul(
                        paux[0:32, 0:256], buf[:, k, :], xnq[:, k, 0:256],
                        start=(k == 0), stop=(k == 15)))
                AC.wait(TE, mt)
                ea_tag = AC.opi(lambda e: e.activation(
                    out=ea_q[:], in_=paux[0:32, 0:256], func=AF.Exp))
                paux_free[0] = (AC, ea_tag)
                wsL.consumed(slot, TE, kb.TE.n)

                if own and q4 == 0 and DBG.get("mode") == "pre0":
                    dbg_dump(kb, "xres", C.x_res[:, 0:2, :], [128, 2, DM], F32)
                    dbg_dump(kb, "stats", stats.ap(), [128, 32], F32)
                    dbg_dump(kb, "xt", xt_q.ap(), [128, 2, DI], BF16)
                    dbg_dump(kb, "zt", zt_q.ap(), [128, 2, DI], BF16)
                    dbg_dump(kb, "bq", b_q.ap(), [128, HP, 256], BF16)
                    dbg_dump(kb, "cq", c_q.ap(), [128, HP, 256], BF16)
                    dbg_dump(kb, "ea", ea_q.ap(), [32, 256], F32)
                    dbg_dump(kb, "xnq", xnq.ap(), [128, NKD, 256], BF16)
                    kb.flush()
                    return
                for tl in range(2):
                    if paux_free[0] is not None:
                        TE.wait(*paux_free[0])
                    TE.wait(AC, ea_tag)
                    t1 = TE.opi(lambda e, tl=tl: e.transpose(
                        paux[:, 0:32], ea_q[:, 128*tl:128*tl+128],
                        C.idf[0:32, 0:32]))
                    DV.wait(TE, t1)
                    d1 = DV.opi(lambda e: e.tensor_copy(ea_tok[:], paux[:, 0:32]))
                    TE.wait(DV, d1)
                    TE.wait_dma(SY)
                    TE.opi(lambda e: e.matmul(paux[:, 32:64], C.uneg[:],
                                              ea_tok[:], start=True, stop=True))
                    t3 = TE.opi(lambda e: e.matmul(paux[:, 64:96], C.negones[:],
                                                   ea_tok[:], start=True, stop=True))
                    DV.wait(TE, t3)
                    DV.opi(lambda e: e.tensor_copy(g_sb[:], paux[:, 32:64]))
                    DV.self_edge()
                    d3 = DV.opi(lambda e: e.tensor_sub(w_sb[:], paux[:, 64:96],
                                                       g_sb[:]))
                    AC.wait(DV, d3)
                    AC.opi(lambda e: e.activation(out=w_sb[:], in_=w_sb[:],
                                                  func=AF.Exp))
                    AC.opi(lambda e: e.activation(out=eg_sb[:], in_=g_sb[:],
                                                  func=AF.Exp))
                    a3 = AC.opi(lambda e: e.activation(out=eG_sb[:],
                                                       in_=paux[:, 64:96],
                                                       func=AF.Exp))
                    paux_free[0] = (AC, a3)
                    AC.self_edge()
                    DV.self_edge()
                    DV.wait(AC, a3)

                    bT_ready = 0
                    for h in range(H):
                        hp, ho = h // 2, 64 * (h % 2)
                        bf = h % 2
                        if h % 2 == 0:
                            q = pbt_state.get("free", (None, 0))
                            if q[0] is not None:
                                TE.wait(*q)
                            TE.wait(DV, bc_dv)
                            tb = TE.opi(lambda e, hp=hp, tl=tl: e.transpose(
                                pbt[:, 0:128], b_q[:, hp, 128*tl:128*tl+128],
                                C.idb[:]))
                            DV.wait(TE, tb)
                            bT_ready = DV.opi(lambda e: e.tensor_copy(
                                bT_sb[:], pbt[:, 0:128]))
                            pbt_state["free"] = (DV, bT_ready)
                        if own:
                            if psc_free[0] is not None:
                                TE.wait(*psc_free[0])
                            ms = TE.opi(lambda e, hp=hp, ho=ho, tl=tl: e.matmul(
                                psc[:], b_q[ho:ho+64, hp, 128*tl:128*tl+128],
                                c_q[ho:ho+64, hp, 128*tl:128*tl+128],
                                start=True, stop=True))
                            q = sbf.get("gs", (None, 0))
                            if q[0] is not None:
                                DV.wait(*q)
                            dg = DV.opi(lambda e, h=h: e.tensor_copy(
                                gs_tile[:],
                                g_sb[:, h:h+1].to_broadcast([128, 128])))
                            if pgt_free[0] is not None:
                                TE.wait(*pgt_free[0])
                            TE.wait(DV, dg)
                            tg = TE.opi(lambda e: e.transpose(
                                pgt[:], gs_tile[:], C.idf[:]))
                            sbf["gs"] = (TE, tg)
                            q = sbf.get(f"dec{bf}", (None, 0))
                            if q[0] is not None:
                                DV.wait(*q)
                            DV.wait(TE, tg)
                            da = DV.opi(lambda e, h=h, bf=bf:
                                        e.scalar_tensor_tensor(
                                dec_sb[bf][:], pgt[:], g_sb[:, h:h+1],
                                C.maskt[:], op0=ALU.subtract, op1=ALU.add))
                            pgt_free[0] = (DV, da)
                            AC.wait(DV, da)
                            ax = AC.opi(lambda e, bf=bf: e.activation(
                                out=dec_sb[bf][:], in_=dec_sb[bf][:],
                                func=AF.Exp))
                            q = sbf.get(f"smt{bf}", (None, 0))
                            if q[0] is not None:
                                DV.wait(*q)
                            DV.wait(AC, ax)
                            DV.wait(TE, ms)
                            dm = DV.opi(lambda e, bf=bf: e.tensor_mul(
                                smt_sb[bf][:], psc[:], dec_sb[bf][:]))
                            psc_free[0] = (DV, dm)
                            sbf[f"dec{bf}"] = (DV, dm)
                            if py_free[0] is not None:
                                TE.wait(*py_free[0])
                            TE.wait(DV, dm)
                            TE.wait(DV, xz_dv)
                            TE.opi(lambda e, bf=bf, hp=hp, ho=ho, tl=tl:
                                   e.matmul(
                                py[:, ho:ho+64], smt_sb[bf][:],
                                xt_q[:, tl, 128*hp+ho:128*hp+ho+64],
                                start=True, stop=True))
                            TE.wait(DV, S_ready[0])
                            my2 = TE.opi(lambda e, hp=hp, ho=ho, tl=tl:
                                         e.matmul(
                                py[:, 128+ho:128+ho+64],
                                c_q[ho:ho+64, hp, 128*tl:128*tl+128],
                                S_b[ho:ho+64, hp, :], start=True, stop=True))
                            sbf[f"smt{bf}"] = (TE, my2)
                            AC.wait(TE, my2)
                            ay = AC.opi(lambda e, bf=bf, h=h, ho=ho:
                                        e.activation(
                                out=t64[bf][:], in_=py[:, 128+ho:128+ho+64],
                                func=AF.Copy, scale=eg_sb[:, h:h+1]))
                            DV.wait(AC, ay)
                            dy = DV.opi(lambda e, bf=bf, hp=hp, ho=ho:
                                        e.scalar_tensor_tensor(
                                y_sb[:, 128*hp+ho:128*hp+ho+64],
                                py[:, ho:ho+64], 1.0, t64[bf][:],
                                op0=ALU.mult, op1=ALU.add))
                            py_free[0] = (DV, dy)
                        q = sbf.get(f"xw{bf}", (None, 0))
                        if q[0] is not None:
                            DV.wait(*q)
                        dx = DV.opi(lambda e, bf=bf, hp=hp, ho=ho, tl=tl, h=h:
                                    e.tensor_scalar(
                            xw_sb[bf][:], xt_q[:, tl, 128*hp+ho:128*hp+ho+64],
                            w_sb[:, h:h+1], None, op0=ALU.mult))
                        if pds_free[0] is not None:
                            TE.wait(*pds_free[0])
                        TE.wait(DV, dx)
                        TE.wait(DV, bT_ready)
                        md = TE.opi(lambda e, bf=bf, ho=ho: e.matmul(
                            pds[ho:ho+64, :], bT_sb[:, ho:ho+64], xw_sb[bf][:],
                            start=True, stop=True))
                        sbf[f"xw{bf}"] = (TE, md)
                        DV.wait(TE, md)
                        DV.opi(lambda e, hp=hp, ho=ho, h=h:
                               e.scalar_tensor_tensor(
                            S_b[ho:ho+64, hp, :], S_f[ho:ho+64, hp, :],
                            eG_sb[ho:ho+64, h:h+1], pds[ho:ho+64, :],
                            op0=ALU.mult, op1=ALU.add))
                        ds1 = DV.opi(lambda e, hp=hp, ho=ho, h=h:
                                     e.scalar_tensor_tensor(
                            S_f[ho:ho+64, hp, :], S_f[ho:ho+64, hp, :],
                            eG_sb[ho:ho+64, h:h+1], pds[ho:ho+64, :],
                            op0=ALU.mult, op1=ALU.add))
                        pds_free[0] = (DV, ds1)
                        S_ready[0] = ds1

                    if own:
                        DV.wait_dma(GP)
                        DV.opi(lambda e, tl=tl: e.tensor_mul(
                            dtmp[:], xt_q[:, tl, :], Db[:]))
                        DV.opi(lambda e: e.tensor_add(y_sb[:], y_sb[:], dtmp[:]))
                        DV.wait(AC, xz_ac)
                        DV.opi(lambda e, tl=tl: e.tensor_mul(
                            y_sb[:], y_sb[:], zt_q[:, tl, :]))
                        dvy = DV.opi(lambda e: e.reduce_sum(
                            stats[:, 0:1], y_sb[:], axis=AX.X))
                        AC.wait(DV, dvy)
                        acq = AC.opi(lambda e: e.activation(
                            out=dtmp[:], in_=y_sb[:], func=AF.Square,
                            accum_out=ssqt[:, 0:1]))
                        DV.wait(AC, acq)
                        DV.opi(lambda e: e.tensor_scalar(
                            stats[:, 2:3], stats[:, 0:1], 1.0/DI, None,
                            op0=ALU.mult))
                        DV.self_edge()
                        DV.opi(lambda e: e.tensor_mul(
                            stats[:, 0:1], stats[:, 2:3], stats[:, 2:3]))
                        DV.self_edge()
                        vq = DV.opi(lambda e: e.scalar_tensor_tensor(
                            stats[:, 1:2], ssqt[:, 0:1], 1.0/DI,
                            stats[:, 0:1], op0=ALU.mult, op1=ALU.subtract))
                        AC.wait(DV, vq)
                        AC.opi(lambda e: e.activation(
                            out=stats[:, 1:2], in_=stats[:, 1:2], func=AF.Ln,
                            bias=EPS))
                        AC.self_edge()
                        rs = AC.opi(lambda e: e.activation(
                            out=stats[:, 3:4], in_=stats[:, 1:2], func=AF.Exp,
                            scale=-0.5))
                        DV.wait(AC, rs)
                        DV.self_edge()
                        DV.opi(lambda e: e.tensor_scalar(
                            y_sb[:], y_sb[:], stats[:, 2:3], stats[:, 3:4],
                            op0=ALU.subtract, op1=ALU.mult))
                        DV.opi(lambda e: e.tensor_mul(y_sb[:], y_sb[:], gb[:]))
                        ynt = DV.opi(lambda e: e.tensor_add(
                            dtmp[:], y_sb[:], bb[:]))
                        for b in range(4):
                            q = pbt_state.get("free", (None, 0))
                            if q[0] is not None:
                                TE.wait(*q)
                            TE.wait(DV, ynt)
                            for j in range(4):
                                k = 4*b + j
                                tr = TE.opi(lambda e, k=k, j=j: e.transpose(
                                    pbt[:, 128*j:128*j+128],
                                    dtmp[:, 128*k:128*k+128], C.idb[:]))
                            DV.wait(TE, tr)
                            cp = DV.opi(lambda e, b=b, tl=tl: e.tensor_copy(
                                yn_q[:, 4*b:4*b+4, 128*tl:128*tl+128],
                                pbt.ap().rearrange("p (j c) -> p j c", j=4)))
                            pbt_state["free"] = (DV, cp)
                    if own and q4 == 0 and tl == 0 and \
                            DBG.get("mode") == "scan0":
                        dbg_dump(kb, "ysb", y_sb.ap(), [128, DI], F32)
                        dbg_dump(kb, "gsb", g_sb.ap(), [128, 32], F32)
                        dbg_dump(kb, "wsb", w_sb.ap(), [128, 32], F32)
                        dbg_dump(kb, "egsb", eg_sb.ap(), [128, 32], F32)
                        dbg_dump(kb, "eGsb", eG_sb.ap(), [128, 32], F32)
                        dbg_dump(kb, "Sf", S_f.ap(), [128, HP, ND], F32)
                        dbg_dump(kb, "yn", dtmp.ap(), [128, DI], BF16)
                        kb.flush()
                        return
                yn_dv = kb.DV.n

                if own:
                    for n in range(4):
                        subs = [wsA.load(
                            woutaT3[:, 8*kh:8*kh+8, 512*n:512*n+512])
                            for kh in range(2)]
                        for tl in range(2):
                            tt = 2*q4 + tl
                            bnk = tl % 2
                            qf = pxz_free[bnk]
                            if qf[0] is not None:
                                TE.wait(*qf)
                            TE.wait(DV, yn_dv)
                            mt = 0
                            for k in range(16):
                                kh, kj = k // 8, k % 8
                                tag, buf, slot = subs[kh]
                                TE.wait(*tag)
                                mt = TE.opi(lambda e, k=k, kj=kj, buf=buf,
                                            tl=tl, bnk=bnk: e.matmul(
                                    pxz[bnk][:], yn_q[:, k, 128*tl:128*tl+128],
                                    buf[:, kj, :], start=(k == 0),
                                    stop=(k == 15)))
                            DV.wait(TE, mt)
                            et = DV.opi(lambda e, tt=tt, n=n, bnk=bnk:
                                        e.tensor_add(
                                C.x_res[:, tt, 512*n:512*n+512],
                                C.x_res[:, tt, 512*n:512*n+512], pxz[bnk][:]))
                            pxz_free[bnk] = (DV, et)
                        for tag, buf, slot in subs:
                            wsA.consumed(slot, TE, kb.TE.n)
                if own and q4 == 0 and DBG.get("mode") == "quarter0":
                    dbg_dump(kb, "xr2", C.x_res[:, 0:2, :], [128, 2, DM], F32)
                    dbg_dump(kb, "ynq", yn_q.ap(), [128, NKD, 256], BF16)
                    kb.flush()
                    return
        kb.barrier()
        kb.flush()


def emit_ffn(kb, C, call):
    nc = kb.nc
    TE, DV, AC, SY, GP = kb.TE, kb.DV, kb.AC, kb.SY, kb.GP
    with ExitStack() as actx:
        def sb(name, shape, dtype):
            return actx.enter_context(
                nc.sbuf_tensor(f"f{call}_{name}", shape, dtype))
        kb.ctx_sb = sb

        def ps(name, shape, dtype=F32):
            return actx.enter_context(
                nc.psum_tensor(f"f{call}_{name}", shape, dtype))

        xn_fm = sb("xn_fm", [128, NKD, T], BF16)
        sq = sb("sq", [128, DM], F32)
        stats = sb("stats", [128, 32], F32)
        ssqt = sb("ssqt", [128, 8], F32)
        p_sb = sb("p_sb", [128, 32], F32)
        top8 = sb("top8", [128, 8], F32)
        wm = sb("wm", [128, 32], F32)
        silu_sb = sb("silu", [128, 1024], BF16)
        gkern_sb = sb("gkern_sb", [128, 16, NE], BF16)
        w_rout = sb("w_rout", [128, NT, NE], F32)
        gk_tag = GP.opi(lambda e: e.dma_start(
            out=gkern_sb[:],
            in_=C.gkern.rearrange("(kk p) m -> p kk m", p=128)))
        I_fm = [sb(f"I{i}", [128, NKE, T], BF16) for i in range(2)]
        pbt = ps("pbt", [128, 512], BF16)
        pbt_state = {}
        pG = ps("pG", [128, 1024])
        pU = ps("pU", [128, 1024])
        prt = ps("prt", [128, 32])
        pd = [ps(f"pd{i}", [128, 512]) for i in range(2)]
        wsGU = WStream(kb, "wsGU", [128, 16, 256], BF16, 4)
        wsD = WStream(kb, "wsD", [128, NKE, 1024], BF16, 2)

        xr = (DV, kb.DV.n)
        xn_tag = emit_ln(kb, C, C.x_res, list(range(NT)), xn_fm,
                         lambda tt: 128 * tt, xr, sq, stats, ssqt, pbt,
                         pbt_state)

        prt_free = (None, 0)
        for tt in range(NT):
            if prt_free[0] is not None:
                TE.wait(*prt_free)
            TE.wait(DV, xn_tag)
            TE.wait_dma(GP)
            mt = 0
            for k in range(16):
                mt = TE.opi(lambda e, k=k, tt=tt: e.matmul(
                    prt[:], xn_fm[:, k, 128*tt:128*tt+128],
                    gkern_sb[:, k, :], start=(k == 0), stop=(k == 15)))
            DV.wait(TE, mt)
            d1 = DV.opi(lambda e: e.reduce_max(stats[:, 0:1], prt[:],
                                               axis=AX.X, negate=True))
            AC.wait(DV, d1)
            a1 = AC.opi(lambda e: e.activation(out=p_sb[:], in_=prt[:],
                                               func=AF.Exp, bias=stats[:, 0:1]))
            prt_free = (AC, a1)
            DV.wait(AC, a1)
            DV.opi(lambda e: e.max(out=top8[:], in_=p_sb[:]))
            DV.self_edge()
            DV.opi(lambda e: e.tensor_scalar(wm[:], p_sb[:], top8[:, 7:8],
                                             None, op0=ALU.is_ge))
            DV.self_edge()
            DV.opi(lambda e: e.tensor_mul(wm[:], wm[:], p_sb[:]))
            DV.self_edge()
            DV.opi(lambda e: e.reduce_sum(stats[:, 1:2], wm[:], axis=AX.X))
            DV.self_edge()
            DV.opi(lambda e: e.reciprocal(stats[:, 2:3], stats[:, 1:2]))
            DV.self_edge()
            DV.opi(lambda e, tt=tt: e.tensor_scalar(
                w_rout[:, tt, :], wm[:], stats[:, 2:3], None, op0=ALU.mult))
        DV.self_edge()
        w_dv = kb.DV.n

        pGU_free = (None, 0)
        pd_free = [(None, 0), (None, 0)]
        I_free = [(None, 0), (None, 0)]
        for e_i in range(NE):
            Ib = e_i % 2
            for mb in range(3):
                subs = {}
                for gu in range(2):
                    wsrc = C.gproj if gu == 0 else C.uproj
                    subs[gu] = wsGU.load(
                        wsrc[e_i].rearrange("(kk p) m -> p kk m", p=128)[
                            :, :, 256*mb:256*mb+256])
                for m_in in range(2):
                    m = 2*mb + m_in
                    gtag = utag = 0
                    for gu in range(2):
                        tag, buf, slot = subs[gu]
                        pp = pG if gu == 0 else pU
                        if pGU_free[0] is not None:
                            TE.wait(*pGU_free)
                        TE.wait(*tag)
                        mt = 0
                        for k in range(16):
                            for n in range(2):
                                mt = TE.opi(lambda e, k=k, n=n, m_in=m_in,
                                            buf=buf, pp=pp: e.matmul(
                                    pp[:, 512*n:512*n+512],
                                    buf[:, k, 128*m_in:128*m_in+128],
                                    xn_fm[:, k, 512*n:512*n+512],
                                    start=(k == 0), stop=(k == 15)))
                        if gu == 0:
                            gtag = mt
                        else:
                            utag = mt
                    AC.wait(TE, gtag)
                    a1 = AC.opi(lambda e: e.activation(
                        out=silu_sb[:], in_=pG[:], func=AF.Silu))
                    q = I_free[Ib]
                    if q[0] is not None:
                        DV.wait(*q)
                    DV.wait(AC, a1)
                    DV.wait(TE, utag)
                    d1 = DV.opi(lambda e, Ib=Ib, m=m: e.tensor_mul(
                        I_fm[Ib][:, m, :], silu_sb[:], pU[:]))
                    pGU_free = (DV, d1)
                for gu in range(2):
                    tag, buf, slot = subs[gu]
                    wsGU.consumed(slot, TE, kb.TE.n)
            I_ready = kb.DV.n
            for nh in range(2):
                tag, buf, slot = wsD.load(
                    C.dproj[e_i].rearrange("(kk p) m -> p kk m", p=128)[
                        :, :, 1024*nh:1024*nh+1024])
                for tt in range(NT):
                    for nn in range(2):
                        n = 2*nh + nn
                        bnk = (tt * 2 + nn) % 2
                        qf = pd_free[bnk]
                        if qf[0] is not None:
                            TE.wait(*qf)
                        TE.wait(*tag)
                        TE.wait(DV, I_ready)
                        mt = 0
                        for k in range(NKE):
                            mt = TE.opi(lambda e, k=k, tt=tt, nn=nn, buf=buf,
                                        Ib=Ib, bnk=bnk: e.matmul(
                                pd[bnk][:], I_fm[Ib][:, k, 128*tt:128*tt+128],
                                buf[:, k, 512*nn:512*nn+512],
                                start=(k == 0), stop=(k == NKE-1)))
                        DV.wait(TE, mt)
                        et = DV.opi(lambda e, tt=tt, n=n, e_i=e_i, bnk=bnk:
                                    e.scalar_tensor_tensor(
                            C.x_res[:, tt, 512*n:512*n+512], pd[bnk][:],
                            w_rout[:, tt, e_i:e_i+1],
                            C.x_res[:, tt, 512*n:512*n+512],
                            op0=ALU.mult, op1=ALU.add))
                        pd_free[bnk] = (DV, et)
                wsD.consumed(slot, TE, kb.TE.n)
            I_free[Ib] = (TE, kb.TE.n)
        kb.barrier()
        kb.flush()


def emit_attn_b(kb, C):
    nc = kb.nc
    TE, DV, AC, SY, GP = kb.TE, kb.DV, kb.AC, kb.SY, kb.GP
    with ExitStack() as actx:
        def sb(name, shape, dtype):
            return actx.enter_context(
                nc.sbuf_tensor("b_" + name, shape, dtype))
        kb.ctx_sb = sb

        def ps(name, shape, dtype=F32):
            return actx.enter_context(
                nc.psum_tensor("b_" + name, shape, dtype))

        xn_fm = sb("xn_fm", [128, NKD, T], BF16)
        pbt = ps("pbt", [128, 512], BF16)
        pbt_state = {}
        with ExitStack() as lctx:
            sq = lctx.enter_context(nc.sbuf_tensor("b_sq", [128, DM], F32))
            stats = lctx.enter_context(nc.sbuf_tensor("b_stats", [128, 32], F32))
            ssqt = lctx.enter_context(nc.sbuf_tensor("b_ssqt", [128, 8], F32))
            xr = (DV, kb.DV.n)
            xn_tag = emit_ln(kb, C, C.x_res, list(range(NT)), xn_fm,
                             lambda tt: 128 * tt, xr, sq, stats, ssqt, pbt,
                             pbt_state)
            kb.flush()

        xp_h = sb("xp_h", [128, NKB, 512], BF16)
        yz_h = sb("yz_h", [128, NKB, 512], BF16)
        z_sb = [sb(f"z{i}", [128, 512], BF16) for i in range(2)]
        pA = [ps(f"pA{i}", [128, 512]) for i in range(4)]
        pz = ps("pz", [128, 512])
        wsX = WStream(kb, "wsX", [128, 8, 256], BF16, 3)
        wsC = WStream(kb, "wsC", [128, 16, 128], BF16, 3)
        wsO = WStream(kb, "wsO", [128, 8, 512], BF16, 2)

        winbT3 = C.winbT.rearrange("(kk p) m -> p kk m", p=128)
        convT3 = C.convT.rearrange("(kk p) m -> p kk m", p=128)
        woutbT3 = C.woutbT.rearrange("(kk p) m -> p kk m", p=128)

        pA_free = [(None, 0)] * 4
        pz_free = (None, 0)
        z_free = [(None, 0), (None, 0)]
        for half in range(2):
            toff = 512 * half
            for m0 in range(0, NKB, 2):
                subs = [wsX.load(winbT3[:, 8*kh:8*kh+8, 128*m0:128*m0+256])
                        for kh in range(2)]
                for mi in range(2):
                    m = m0 + mi
                    bnk = m % 2
                    qf = pA_free[bnk]
                    if qf[0] is not None:
                        TE.wait(*qf)
                    TE.wait(DV, xn_tag)
                    mt = 0
                    for k in range(16):
                        kh, kj = k // 8, k % 8
                        tag, buf, slot = subs[kh]
                        TE.wait(*tag)
                        mt = TE.opi(lambda e, kj=kj, k=k, buf=buf, bnk=bnk,
                                    toff=toff, mi=mi: e.matmul(
                            pA[bnk][:], buf[:, kj, 128*mi:128*mi+128],
                            xn_fm[:, k, toff:toff+512],
                            start=(k == 0), stop=(k == 15)))
                    DV.wait(TE, mt)
                    et = DV.opi(lambda e, m=m, bnk=bnk: e.tensor_copy(
                        xp_h[:, m, :], pA[bnk][:]))
                    pA_free[bnk] = (DV, et)
                for tag, buf, slot in subs:
                    wsX.consumed(slot, TE, kb.TE.n)
            xp_dv = kb.DV.n

            zsubs_pair = {}
            for m in range(NKB):
                csubs = [wsC.load(convT3[:, 16*kh:16*kh+16, 128*m:128*m+128])
                         for kh in range(2)]
                if m % 2 == 0:
                    zsubs_pair = {kh: wsX.load(
                        winbT3[:, 8*kh:8*kh+8, 4096+128*m:4096+128*m+256])
                        for kh in range(2)}
                zsubs = [zsubs_pair[kh] for kh in range(2)]
                zmi = m % 2
                bnk = m % 2
                qf = pA_free[bnk]
                if qf[0] is not None:
                    TE.wait(*qf)
                TE.wait(DV, xp_dv)
                mt = 0
                for k in range(32):
                    kh, kj = k // 16, k % 16
                    tag, buf, slot = csubs[kh]
                    TE.wait(*tag)
                    mt = TE.opi(lambda e, kj=kj, k=k, buf=buf, bnk=bnk: e.matmul(
                        pA[bnk][:], buf[:, kj, :], xp_h[:, k, :],
                        start=(k == 0), stop=(k == 31)))
                if pz_free[0] is not None:
                    TE.wait(*pz_free)
                mz = 0
                for k in range(16):
                    kh, kj = k // 8, k % 8
                    tag, buf, slot = zsubs[kh]
                    TE.wait(*tag)
                    mz = TE.opi(lambda e, kj=kj, k=k, buf=buf, toff=toff,
                                zmi=zmi: e.matmul(
                        pz[:], buf[:, kj, 128*zmi:128*zmi+128],
                        xn_fm[:, k, toff:toff+512],
                        start=(k == 0), stop=(k == 15)))
                zb = m % 2
                q = z_free[zb]
                if q[0] is not None:
                    AC.wait(*q)
                AC.wait(TE, mz)
                az = AC.opi(lambda e, zb=zb: e.activation(
                    out=z_sb[zb][:], in_=pz[:], func=AF.Silu))
                pz_free = (AC, az)
                DV.wait(AC, az)
                DV.wait(TE, mt)
                dy = DV.opi(lambda e, m=m, bnk=bnk, zb=zb:
                            e.scalar_tensor_tensor(
                    yz_h[:, m, :], pA[bnk][:], C.DbT_sb[:, m:m+1],
                    z_sb[zb][:], op0=ALU.mult, op1=ALU.mult))
                pA_free[bnk] = (DV, dy)
                z_free[zb] = (DV, dy)
                for tag, buf, slot in csubs:
                    wsC.consumed(slot, TE, kb.TE.n)
                if m % 2 == 1:
                    for tag, buf, slot in zsubs:
                        wsX.consumed(slot, TE, kb.TE.n)
            yz_dv = kb.DV.n

            for n in range(4):
                for b4 in range(4):
                    qf = pA_free[b4]
                    if qf[0] is not None:
                        TE.wait(*qf)
                TE.wait(DV, yz_dv)
                subs = {0: wsO.load(woutbT3[:, 0:8, 512*n:512*n+512]),
                        1: wsO.load(woutbT3[:, 8:16, 512*n:512*n+512])}
                mt = 0
                for kq in range(4):
                    tag, buf, slot = subs[kq]
                    TE.wait(*tag)
                    for tl in range(4):
                        for kj in range(8):
                            k = 8*kq + kj
                            mt = TE.opi(lambda e, kj=kj, k=k, buf=buf, tl=tl:
                                        e.matmul(
                                pA[tl][:], yz_h[:, k, 128*tl:128*tl+128],
                                buf[:, kj, :], start=(k == 0), stop=(k == 31)))
                    wsO.consumed(slot, TE, kb.TE.n)
                    if kq + 2 < 4:
                        subs[kq + 2] = wsO.load(
                            woutbT3[:, 8*(kq+2):8*(kq+2)+8, 512*n:512*n+512])
                for tl in range(4):
                    tt = 4*half + tl
                    DV.wait(TE, mt)
                    et = DV.opi(lambda e, tt=tt, n=n, tl=tl: e.tensor_add(
                        C.x_res[:, tt, 512*n:512*n+512],
                        C.x_res[:, tt, 512*n:512*n+512], pA[tl][:]))
                    pA_free[tl] = (DV, et)
        kb.barrier()
        kb.flush()


class Consts:
    pass


def build_program(stage="full"):
    nc = bass.Bass("TRN2", target_bir_lowering=False, debug=False)
    C = Consts()
    dt = nc.dram_tensor
    C.x_own = dt("x_own", [T, DM], F32, kind="ExternalInput").ap()
    C.x_part = dt("x_part", [T, DM], F32, kind="ExternalInput").ap()
    C.winaT = dt("winaT", [DM, 8224], F32, kind="ExternalInput").ap()
    C.woutaT = dt("woutaT", [DI, DM], F32, kind="ExternalInput").ap()
    C.gammaB = dt("gammaB", [128, DI], F32, kind="ExternalInput").ap()
    C.betaB = dt("betaB", [128, DI], F32, kind="ExternalInput").ap()
    C.DaB = dt("DaB", [128, DI], F32, kind="ExternalInput").ap()
    C.gkern = dt("gkern", [DM, NE], F32, kind="ExternalInput").ap()
    C.gproj = dt("gproj", [NE, DM, EM], F32, kind="ExternalInput").ap()
    C.uproj = dt("uproj", [NE, DM, EM], F32, kind="ExternalInput").ap()
    C.dproj = dt("dproj", [NE, EM, DM], F32, kind="ExternalInput").ap()
    C.winbT = dt("winbT", [DM, 2*BI], F32, kind="ExternalInput").ap()
    C.convT = dt("convT", [BI, BI], F32, kind="ExternalInput").ap()
    C.DbT = dt("DbT", [128, NKB], F32, kind="ExternalInput").ap()
    C.woutbT = dt("woutbT", [BI, DM], F32, kind="ExternalInput").ap()
    C.c_idf = dt("c_idf", [128, 128], F32, kind="ExternalInput").ap()
    C.c_idb = dt("c_idb", [128, 128], BF16, kind="ExternalInput").ap()
    C.c_uneg = dt("c_uneg", [128, 128], F32, kind="ExternalInput").ap()
    C.c_negones = dt("c_negones", [128, 128], F32, kind="ExternalInput").ap()
    C.c_maskt = dt("c_maskt", [128, 128], F32, kind="ExternalInput").ap()
    C.c_vals = dt("c_vals", [128, 8], F32, kind="ExternalInput").ap()
    C.out = dt("out", [T, DM], F32, kind="ExternalOutput").ap()

    with ExitStack() as ctx:
        block = ctx.enter_context(nc.Block())
        kb = KB(nc, ctx, block)
        C.x_res = kb.sb("x_res", [128, NT, DM], F32)
        C.xn_tok = kb.sb("xn_tok", [128, DM], BF16)
        C.idb = kb.sb("idb", [128, 128], BF16)
        C.DbT_sb = kb.sb("DbT_sb", [128, NKB], F32)
        C.vals = kb.sb("vals", [128, 8], F32)
        for j, v in enumerate(CONST_VALS):
            nc.const_aps.register(F32, v, C.vals[:, j:j+1])

        for q, nm in ((kb.TE, "sTE"), (kb.DV, "sDV"), (kb.AC, "sAC"),
                      (kb.SY, "sSY"), (kb.GP, "sGP")):
            q.bind(ctx.enter_context(nc.semaphore(nm)))

        SY, GP = kb.SY, kb.GP
        C.x_ready = (SY, SY.opi(lambda e: e.dma_start(
            out=C.x_res[:], in_=C.x_own.rearrange("(tt p) d -> p tt d", p=128))))
        ct = 0
        for src, dst in ((C.c_idb, C.idb), (C.DbT, C.DbT_sb),
                         (C.c_vals, C.vals)):
            ct = SY.opi(lambda e, src=src, dst=dst: e.dma_start(
                out=dst[:], in_=src))
        C.const_tag = ct
        kb.flush()

        emit_attn_a(kb, C)
        if stage in ("ffn1", "attnb", "full"):
            emit_ffn(kb, C, 0)
        if stage in ("attnb", "full"):
            emit_attn_b(kb, C)
        if stage == "full":
            emit_ffn(kb, C, 1)

        st = SY.opi(lambda e: e.dma_start(
            out=C.out.rearrange("(tt p) d -> p tt d", p=128), in_=C.x_res[:]))
        SY.op(lambda e, st=st: e.wait_ge(kb.SY.sem, st))
        kb.flush()
    return nc


def _host_consts():
    import ml_dtypes
    idf = np.eye(128, dtype=np.float32)
    idb = np.eye(128, dtype=np.float32).astype(ml_dtypes.bfloat16)
    s = np.arange(128)
    uneg = -(s[:, None] <= s[None, :]).astype(np.float32)
    negones = -np.ones((128, 128), dtype=np.float32)
    maskt = np.where(s[:, None] <= s[None, :], 0.0, NEG).astype(np.float32)
    return idf, idb, uneg, negones, maskt


def make_in_maps(inputs, n_cores=8):
    f32 = np.float32
    asc = np.ascontiguousarray
    x = np.asarray(inputs["L0_attn_a_x"], f32)
    winaT = asc(np.asarray(inputs["L0_attn_a_W_in"], f32).T)
    woutaT = asc(np.asarray(inputs["L0_attn_a_W_out"], f32).T)
    gammaB = asc(np.broadcast_to(
        np.asarray(inputs["L0_attn_a_gamma"], f32).reshape(1, DI), (128, DI)))
    betaB = asc(np.broadcast_to(
        np.asarray(inputs["L0_attn_a_beta"], f32).reshape(1, DI), (128, DI)))
    DaB = asc(np.broadcast_to(
        np.asarray(inputs["L0_attn_a_D"], f32).reshape(1, DI), (128, DI)))
    gkern = asc(np.asarray(inputs["L0_ffn_gate_kernel"], f32))
    gproj = asc(np.asarray(inputs["L0_ffn_gate_proj"], f32))
    uproj = asc(np.asarray(inputs["L0_ffn_up_proj"], f32))
    dproj = asc(np.asarray(inputs["L0_ffn_down_proj"], f32))
    winbT = asc(np.asarray(inputs["L0_attn_b_W_in"], f32).T)
    convT = asc(np.asarray(inputs["L0_attn_b_conv_w"], f32).T)
    DbT = asc(np.asarray(inputs["L0_attn_b_D"], f32).reshape(NKB, 128).T)
    woutbT = asc(np.asarray(inputs["L0_attn_b_W_out"], f32).T)
    idf, idb, uneg, negones, maskt = _host_consts()
    cvals = np.zeros((128, 8), dtype=f32)
    cvals[:, :len(CONST_VALS)] = CONST_VALS
    zeros = np.zeros((T, DM), dtype=f32)

    shared = dict(winaT=winaT, woutaT=woutaT, gammaB=gammaB, betaB=betaB, DaB=DaB,
                  gkern=gkern, gproj=gproj, uproj=uproj, dproj=dproj,
                  winbT=winbT, convT=convT, DbT=DbT, woutbT=woutbT,
                  c_idf=idf, c_idb=idb, c_uneg=uneg, c_negones=negones,
                  c_maskt=maskt, c_vals=cvals)
    in_maps = []
    for c in range(n_cores):
        b, hhalf = c // 2, c % 2
        m = dict(shared)
        m["x_own"] = asc(x[b, 1024*hhalf:1024*(hhalf+1), :])
        m["x_part"] = asc(x[b, 0:1024, :]) if hhalf == 1 else zeros
        in_maps.append(m)
    return in_maps


def kernel(**inputs):
    _ = (inputs["L0_attn_b_W_x"], inputs["L0_attn_b_W_dt"],
         inputs["L0_attn_b_b_dt"])
    n_cores = 8
    nc = build_program("full")
    in_maps = make_in_maps(inputs, n_cores)
    res = run_bass_kernel_spmd(nc, in_maps, core_ids=list(range(n_cores)))
    out = np.empty((4, 2048, 2048), dtype=np.float32)
    for c in range(n_cores):
        b, hhalf = c // 2, c % 2
        out[b, 1024*hhalf:1024*(hhalf+1), :] = res.results[c]["out"]
    return out
